# version 06
# baseline (speedup 1.0000x reference)
import numpy as np
import ml_dtypes
import concourse.bass as bass
import concourse.mybir as mybir
from concourse.bass_utils import run_bass_kernel_spmd

F32 = mybir.dt.float32
BF16 = mybir.dt.bfloat16
I32 = mybir.dt.int32
AF = mybir.ActivationFunctionType
ALU = mybir.AluOpType
AX = mybir.AxisListType
BF = ml_dtypes.bfloat16

NCORES = 8
NTOK = 32768
TPC = NTOK // NCORES
T = 512
NTILES = TPC // T
DM = 1024
FF = 4096
NCH = DM // 128
NFH = FF // 128
LAMBDA_INIT = 0.8 - 0.6 * float(np.exp(-0.3 * 1.0))
SCALE_Q = float(32 ** -0.5)
EPS = 1e-5
MAGIC = 512.5


def _perm_qk():
    p = np.zeros(DM, dtype=np.int64)
    for i in range(32):
        for d in range(32):
            p[32 * i + d] = (i % 16) * 64 + 2 * d + (i // 16)
    return p

def _pad_v(Wv):
    out = np.zeros((2 * DM, DM), dtype=np.float32)
    for dd in range(64):
        for h in range(16):
            out[32 * dd + h, :] = Wv[64 * h + dd, :]
    return out

def _pad_o(Wo):
    out = np.zeros((2 * DM, DM), dtype=np.float32)
    for dd in range(64):
        for a in range(16):
            out[32 * dd + a, :] = Wo[:, 64 * a + dd]
    return out


def _stage(inputs):
    g = lambda k: np.asarray(inputs[k], dtype=np.float32)
    x = g("L0_attn_x").reshape(NTOK, DM)
    xfm = np.ascontiguousarray(x.T)
    Wq = g("L0_attn_Wq")[_perm_qk(), :]
    Wk = g("L0_attn_Wk")[_perm_qk(), :]
    WvP = _pad_v(g("L0_attn_Wv"))
    wqkvT = np.ascontiguousarray(np.concatenate([Wq, Wk, WvP], axis=0).T)
    woT = np.ascontiguousarray(_pad_o(g("L0_attn_Wo")))
    w1T = np.ascontiguousarray(g("L0_attn_W1").T)
    w2T = np.ascontiguousarray(g("L0_attn_W2").T)
    f1T = np.ascontiguousarray(g("L0_ffn_w1").T)
    f2T = np.ascontiguousarray(g("L0_ffn_w2").T)
    vecs = {
        "bo": np.ascontiguousarray(g("L0_attn_bo").reshape(NCH, 128).T),
        "b1a": np.ascontiguousarray(g("L0_attn_b1").reshape(NFH, 128).T),
        "b2a": np.ascontiguousarray(g("L0_attn_b2").reshape(NCH, 128).T),
        "g1": np.ascontiguousarray(g("L0_attn_ln1_g").reshape(NCH, 128).T),
        "bl1": np.ascontiguousarray(g("L0_attn_ln1_b").reshape(NCH, 128).T),
        "g2": np.ascontiguousarray(g("L0_attn_ln2_g").reshape(NCH, 128).T),
        "bl2": np.ascontiguousarray(g("L0_attn_ln2_b").reshape(NCH, 128).T),
        "f1b": np.ascontiguousarray(g("L0_ffn_b1").reshape(NFH, 128).T),
        "f2b": np.ascontiguousarray(g("L0_ffn_b2").reshape(NCH, 128).T),
        "lq1": g("L0_attn_lq1").reshape(1, 32),
        "lq2": g("L0_attn_lq2").reshape(1, 32),
        "lk1": g("L0_attn_lk1").reshape(1, 32),
        "lk2": g("L0_attn_lk2").reshape(1, 32),
    }
    shared = dict(wqkvT=wqkvT, woT=woT, w1T=w1T, w2T=w2T, f1T=f1T, f2T=f2T,
                  onescol=np.ones((128, 1), dtype=BF), onesrow=np.ones((1, 128), dtype=BF),
                  **vecs)
    per_core = [{"xfm": np.ascontiguousarray(xfm[:, c * TPC:(c + 1) * TPC])}
                for c in range(NCORES)]
    return shared, per_core


SERIAL = False

class Sched:
    ENGS = ("tensor", "vector", "scalar", "sync", "gpsimd")

    def __init__(self):
        self.steps = []
        self.count = {e: 0 for e in self.ENGS}
        self.res = {}

    def step(self, engine, fn, reads=(), writes=(), dma=0):
        waits = {}
        if SERIAL:
            waits = {e: c for e, c in self.count.items() if e != engine and c > 0}
        def need(eng, cnt):
            if eng != engine and waits.get(eng, 0) < cnt:
                waits[eng] = cnt
        for r in reads:
            info = self.res.setdefault(r, {"w": None, "r": {}})
            if info["w"] is not None:
                need(*info["w"])
        for w in writes:
            info = self.res.setdefault(w, {"w": None, "r": {}})
            if info["w"] is not None:
                need(*info["w"])
            for eng, cnt in info["r"].items():
                need(eng, cnt)
        inc = 16 * dma if dma else 1
        self.count[engine] += inc
        cnt_after = self.count[engine]
        for r in reads:
            info = self.res[r]
            info["r"][engine] = max(info["r"].get(engine, 0), cnt_after)
        for w in writes:
            self.res[w] = {"w": (engine, cnt_after), "r": {}}
        self.steps.append((engine, fn, waits, inc))

    def make_runner(self, sems):
        per_eng = {e: [] for e in self.ENGS}
        for engine, fn, waits, inc in self.steps:
            per_eng[engine].append((fn, waits, inc))
        def run(eng_name, eng):
            for fn, waits, inc in per_eng[eng_name]:
                for wname, cnt in waits.items():
                    eng.wait_ge(sems[wname], cnt)
                last = fn(eng)
                if inc == 1:
                    if last is not None and hasattr(last, "then_inc"):
                        last.then_inc(sems[eng_name], 1)
                    else:
                        eng.sem_inc(sems[eng_name], 1)
        return run


def build_nc(dbg=None):
    nc = bass.Bass("TRN2", target_bir_lowering=False)
    import contextlib
    es = contextlib.ExitStack()

    d_x = nc.dram_tensor("xfm", [DM, TPC], F32, kind="ExternalInput").ap()
    d_out = nc.dram_tensor("outfm", [DM, TPC], F32, kind="ExternalOutput").ap()
    dw = {}
    for name, shape in [("wqkvT", [DM, 4 * DM]), ("woT", [2 * DM, DM]), ("w1T", [DM, FF]),
                        ("w2T", [FF, DM]), ("f1T", [DM, FF]), ("f2T", [FF, DM])]:
        dw[name] = nc.dram_tensor(name, shape, F32, kind="ExternalInput").ap()
    dv = {}
    for name, shape in [("bo", [128, NCH]), ("b1a", [128, NFH]), ("b2a", [128, NCH]),
                        ("g1", [128, NCH]), ("bl1", [128, NCH]), ("g2", [128, NCH]),
                        ("bl2", [128, NCH]), ("f1b", [128, NFH]), ("f2b", [128, NCH]),
                        ("lq1", [1, 32]), ("lq2", [1, 32]), ("lk1", [1, 32]), ("lk2", [1, 32])]:
        dv[name] = nc.dram_tensor(name, shape, F32, kind="ExternalInput").ap()
    d_ones_c = nc.dram_tensor("onescol", [128, 1], BF16, kind="ExternalInput").ap()
    d_ones_r = nc.dram_tensor("onesrow", [1, 128], BF16, kind="ExternalInput").ap()

    scr = {}
    for name, shape in [("wqkv", [DM, 4 * DM]), ("wo", [2 * DM, DM]), ("w1", [DM, FF]),
                        ("w2", [FF, DM]), ("f1", [DM, FF]), ("f2", [FF, DM])]:
        scr[name] = nc.dram_tensor("s_" + name, shape, BF16).ap()
    d_dbgf = d_dbgb = None
    if dbg is not None:
        d_dbgf = nc.dram_tensor("dbgf", [128, 8, 512], F32, kind="ExternalOutput").ap()
        d_dbgb = nc.dram_tensor("dbgb", [128, 32, 512], BF16, kind="ExternalOutput").ap()

    sb = lambda name, shape, dt=BF16: es.enter_context(nc.sbuf_tensor(name, shape, dt))
    ps = lambda name, shape: es.enter_context(nc.psum_tensor(name, shape, F32))

    bufA = sb("bufA", [128, NCH, T], F32)
    bufB = sb("bufB", [128, NCH, T], F32)
    hbuf = sb("hbuf", [128, NCH, T], BF16)
    xb = sb("xb", [128, NCH, T], BF16)
    xsq = sb("xsq", [128, NCH, T], BF16)
    qfm = sb("qfm", [128, NCH, T], BF16)
    kfm = sb("kfm", [128, NCH, T], BF16)
    vfm = sb("vfm", [128, 16, T], BF16)
    omfm = vfm
    qst = sb("qst", [128, 4096], BF16)
    kst = sb("kst", [128, 4096], BF16)
    vst = sb("vst", [128, 8192], BF16)
    avs = sb("avs", [128, 8192], BF16)
    ebuf = sb("ebuf", [128, 544], BF16)
    entb = sb("entb", [128, 512], BF16)
    entb2 = sb("entb2", [128, 512], BF16)
    at2 = sb("at2", [128, 2, 512], BF16)
    hfm = sb("hfm", [128, NFH, T], BF16)
    wqb = sb("wqb", [128, 8192], BF16)
    statA = sb("statA", [128, 32], F32)
    statB = sb("statB", [128, 24], F32)
    rowb = sb("rowb", [128, 2, T], F32)
    def rb(k):
        if k < 2:
            return rowb[0:1, k, :]
        return y2s[k - 2][0:1, 0:T]
    rowsb = sb("rowsb", [1, 2 * T], BF16)
    rwork = sb("rwork", [1, 64], F32)
    onesc = sb("onesc", [128, 1], BF16)
    onesr = sb("onesr", [1, 128], BF16)
    neglam = sb("neglam", [128, 1], F32)
    ceps = sb("ceps", [128, 1], F32)
    neglam_b = sb("neglam_b", [1, 1], BF16)
    biasq = sb("biasq", [128, 32], F32)
    bias1 = sb("bias1", [128, NFH], F32)
    f1bias = sb("f1bias", [128, NFH], F32)
    f2bias = sb("f2bias", [128, NCH], F32)
    vcol = {}
    for name, w in [("bo", NCH), ("b2a", NCH), ("g1", NCH), ("bl1", NCH), ("g2", NCH),
                    ("bl2", NCH), ("f1b", NFH), ("f2b", NCH), ("b1a", NFH)]:
        vcol[name] = sb("v_" + name, [128, w], F32)
    lrow = sb("lrow", [1, 128], F32)
    y2s = [sb("y2s0", [128, 512], F32), sb("y2s1", [128, 512], F32)]
    i32t = sb("i32t", [128, 2, 512], I32)
    i32a = i32t[:, 0, :]
    i32b = i32t[:, 1, :]
    f32t = sb("f32t", [128, 512], F32)
    f32a = f32t[:, :]
    cgta = y2s[1][:, :]
    esub = y2s[0][:, :]
    wprep = bufA[:, :, :].rearrange("p c t -> p (c t)")
    fprep = bufB[:, :, :].rearrange("p c t -> p (c t)")
    iprep = hfm[:, 0:16, :].rearrange("p c t -> p (c t)").bitcast(I32)
    wprep2 = [hfm[:, 16:24, :].rearrange("p c t -> p (c t)"),
              hfm[:, 24:32, :].rearrange("p c t -> p (c t)")]

    pA0 = ps("pA0", [128, 512])
    pA1 = ps("pA1", [128, 512])
    pCS = ps("pCS", [64, 512])
    pBC0 = ps("pBC0", [128, 512])
    pBC1 = ps("pBC1", [128, 512])
    pAT0 = ps("pAT0", [128, 512])
    pAT1 = ps("pAT1", [128, 512])
    pSM = ps("pSM", [128, 512])

    sems = {}
    S = Sched()
    st = S.step

    def tap(name, ap, tags, cols_f32=None, cols_bf=None, psum=False):
        if dbg != name:
            return
        tgt = d_dbgf if cols_bf is None else d_dbgb
        nparts = ap.shape[0]
        tview = tgt.rearrange("p c t -> p (c t)")[0:nparts, 0:(cols_f32 or cols_bf)]
        if psum:
            def fc(eng, ap=ap):
                return eng.tensor_copy(out=f32t[:, 0:(cols_f32 or cols_bf)], in_=ap)
            st("vector", fc, reads=list(tags), writes=["f32t"])
            ap = f32t[0:nparts, 0:(cols_f32 or cols_bf)]
            tags = ["f32t"]
        def f(eng, ap=ap, tview=tview):
            eng.dma_start(out=tview, in_=ap, allow_slow_non_contiguous=True).then_inc(sems["sync"], 16)
        st("sync", f, reads=list(tags), dma=1)

    def p_vec_in(eng):
        for name in ["bo", "b2a", "g1", "bl1", "g2", "bl2", "f1b", "f2b", "b1a"]:
            eng.dma_start(out=vcol[name][:, :], in_=dv[name][:, :]).then_inc(sems["sync"], 16)
        for j, name in enumerate(["lq1", "lq2", "lk1", "lk2"]):
            eng.dma_start(out=lrow[0:1, 32 * j:32 * j + 32], in_=dv[name][:, :]).then_inc(sems["sync"], 16)
        eng.dma_start(out=onesc[:, :], in_=d_ones_c[:, :]).then_inc(sems["sync"], 16)
        eng.dma_start(out=onesr[:, :], in_=d_ones_r[:, :]).then_inc(sems["sync"], 16)
    st("sync", p_vec_in, writes=["vecs"], dma=15)

    def p_lam1(eng):
        eng.tensor_mul(out=rwork[0:1, 0:32], in0=lrow[0:1, 0:32], in1=lrow[0:1, 64:96])
        eng.tensor_mul(out=rwork[0:1, 32:64], in0=lrow[0:1, 32:64], in1=lrow[0:1, 96:128])
        eng.drain()
        eng.reduce_sum(out=rwork[0:1, 0:1].unsqueeze(2),
                       in_=rwork[0:1, 0:32].unsqueeze(1), axis=AX.X)
        return eng.reduce_sum(out=rwork[0:1, 1:2].unsqueeze(2),
                              in_=rwork[0:1, 32:64].unsqueeze(1), axis=AX.X)
    st("vector", p_lam1, reads=["vecs"], writes=["rwork"])
    def p_lam2(eng):
        return eng.activation(out=rwork[0:1, 2:4], in_=rwork[0:1, 0:2], func=AF.Exp)
    st("scalar", p_lam2, reads=["rwork"], writes=["rwork2"])
    def p_lam3(eng):
        eng.tensor_single_scalar(out=rwork[0:1, 4:5], in_=rwork[0:1, 3:4],
                                 scalar=rwork[0:1, 2:3], op=ALU.subtract)
        eng.drain()
        eng.tensor_scalar_sub(out=rwork[0:1, 4:5], in0=rwork[0:1, 4:5], scalar1=LAMBDA_INIT)
        eng.drain()
        return eng.tensor_copy(out=neglam_b[0:1, 0:1], in_=rwork[0:1, 4:5])
    st("vector", p_lam3, reads=["rwork2"], writes=["neglam_row"])
    tap("rwork", rwork[0:1, 0:64], ["neglam_row"], cols_f32=64)
    tap("lrow", lrow[0:1, 0:128], ["vecs"], cols_f32=128)
    def p_lam4(eng):
        return eng.matmul(pSM[:, 0:1], onesr[0:1, :], neglam_b[0:1, 0:1], start=True, stop=True)
    st("tensor", p_lam4, reads=["neglam_row", "vecs"], writes=["pSM"])
    def p_lam5(eng):
        return eng.tensor_copy(out=neglam[:, :], in_=pSM[:, 0:1])
    st("vector", p_lam5, reads=["pSM"], writes=["neglam"])

    def _round_cols(eng, dst, src, w, pre_scale, post_mul, post_add):
        eng.tensor_scalar(out=y2s[0][:, 0:w], in0=src[:, 0:w], scalar1=pre_scale,
                          scalar2=MAGIC, op0=ALU.mult, op1=ALU.add)
        eng.drain()
        eng.tensor_copy(out=i32a[:, 0:w], in_=y2s[0][:, 0:w])
        eng.drain()
        eng.tensor_copy(out=f32a[:, 0:w], in_=i32a[:, 0:w])
        eng.drain()
        eng.tensor_tensor(out=cgta[:, 0:w], in0=f32a[:, 0:w], in1=y2s[0][:, 0:w], op=ALU.is_gt)
        eng.drain()
        eng.scalar_tensor_tensor(out=f32a[:, 0:w], in0=f32a[:, 0:w], scalar=-512.0,
                                 in1=cgta[:, 0:w], op0=ALU.add, op1=ALU.subtract)
        eng.drain()
        return eng.tensor_scalar(out=dst[:, 0:w], in0=f32a[:, 0:w], scalar1=post_mul,
                                 scalar2=post_add, op0=ALU.mult, op1=ALU.add)
    def p_fbias(eng):
        _round_cols(eng, f1bias, vcol["f1b"], NFH, 64.0, 16.0, 0.0)
        return _round_cols(eng, f2bias, vcol["f2b"], NCH, 64.0, 1.0 / 64.0, 0.0)
    st("vector", p_fbias, reads=["vecs", "f32t", "i32t", "y2s1", "y2s0"],
       writes=["fbias", "f32t", "i32t", "y2s1", "y2s0"])
    tap("f1bias", f1bias[:, :], ["fbias"], cols_f32=32)
    tap("neglam", neglam[:, :], ["neglam"], cols_f32=1)

    for c in range(NCH):
        def p_wi(eng, c=c):
            eng.dma_start(out=wprep[:, 0:4 * DM], in_=dw["wqkvT"][128 * c:128 * c + 128, :]).then_inc(sems["sync"], 16)
        st("sync", p_wi, writes=["bufA"], dma=1)
        def p_bq(eng, c=c):
            last = None
            for o in range(32):
                last = eng.matmul(pSM[:, 8 * o + c:8 * o + c + 1], wprep[:, 128 * o:128 * o + 128],
                                  vcol["bl1"][:, c:c + 1], start=True, stop=True)
            return last
        st("tensor", p_bq, reads=["bufA", "vecs"], writes=["pSM"])
        def p_wc(eng, c=c):
            return eng.activation(out=wprep2[c % 2][:, 0:4 * DM], in_=wprep[:, 0:4 * DM],
                                  func=AF.Copy, scale=vcol["g1"][:, c:c + 1])
        st("scalar", p_wc, reads=["bufA", "vecs", "hfm"], writes=[f"wp2_{c % 2}", "hfm"])
        def p_wo_(eng, c=c):
            eng.dma_start(out=scr["wqkv"][128 * c:128 * c + 128, :],
                          in_=wprep2[c % 2][:, 0:4 * DM]).then_inc(sems["gpsimd"], 16)
        st("gpsimd", p_wo_, reads=[f"wp2_{c % 2}"], writes=[f"sc_wqkv_{c}"], dma=1)
    tap("biasq", biasq[:, 0:32], ["biasq"], cols_f32=32) if False else None
    def p_bq_sum(eng):
        return eng.reduce_sum(out=statA[:, 0:32].unsqueeze(2),
                              in_=pSM[:, 0:256].rearrange("p (o c) -> p o c", c=8),
                              axis=AX.X)
    st("vector", p_bq_sum, reads=["pSM"], writes=["statA"])
    def p_bq_evac(eng):
        eng.activation(out=biasq[:, 0:NCH], in_=statA[:, 0:NCH], func=AF.Copy, scale=SCALE_Q)
        return eng.activation(out=biasq[:, NCH:32], in_=statA[:, NCH:32], func=AF.Copy)
    st("scalar", p_bq_evac, reads=["statA"], writes=["biasq"])
    tap("biasq", biasq[:, 0:32], ["biasq"], cols_f32=32)

    for c in range(NCH):
        def p_wi(eng, c=c):
            eng.dma_start(out=wprep[:, 0:FF], in_=dw["w1T"][128 * c:128 * c + 128, :]).then_inc(sems["sync"], 16)
        st("sync", p_wi, writes=["bufA"], dma=1)
        def p_b1(eng, c=c):
            last = None
            for o in range(NFH):
                last = eng.matmul(pSM[:, 8 * o + c:8 * o + c + 1], wprep[:, 128 * o:128 * o + 128],
                                  vcol["bl2"][:, c:c + 1], start=True, stop=True)
            return last
        st("tensor", p_b1, reads=["bufA", "vecs", "biasq"] if c == 0 else ["bufA", "vecs"],
           writes=["pSM"])
        def p_wc(eng, c=c):
            return eng.activation(out=wprep2[c % 2][:, 0:FF], in_=wprep[:, 0:FF],
                                  func=AF.Copy, scale=vcol["g2"][:, c:c + 1])
        st("scalar", p_wc, reads=["bufA", "vecs", "hfm"], writes=[f"wp2_{c % 2}", "hfm"])
        def p_wo_(eng, c=c):
            eng.dma_start(out=scr["w1"][128 * c:128 * c + 128, :],
                          in_=wprep2[c % 2][:, 0:FF]).then_inc(sems["gpsimd"], 16)
        st("gpsimd", p_wo_, reads=[f"wp2_{c % 2}"], writes=[f"sc_w1_{c}"], dma=1)
    def p_b1_evac(eng):
        eng.reduce_sum(out=statA[:, 0:32].unsqueeze(2),
                       in_=pSM[:, 0:256].rearrange("p (o c) -> p o c", c=8),
                       axis=AX.X)
        eng.drain()
        return eng.tensor_add(out=bias1[:, :], in0=statA[:, 0:32], in1=vcol["b1a"][:, :])
    st("vector", p_b1_evac, reads=["pSM", "vecs", "statA"], writes=["bias1", "statA"])
    tap("bias1", bias1[:, :], ["bias1"], cols_f32=32)

    for name, src_d, nch, wid in [("wo", dw["woT"], 16, DM), ("w2", dw["w2T"], NFH, DM)]:
        for c in range(nch):
            def p_wi(eng, src_d=src_d, c=c, wid=wid):
                eng.dma_start(out=wprep[:, 0:wid], in_=src_d[128 * c:128 * c + 128, :]).then_inc(sems["sync"], 16)
            st("sync", p_wi, writes=["bufA"], dma=1)
            def p_wc(eng, c=c, wid=wid):
                return eng.activation(out=wprep2[c % 2][:, 0:wid], in_=wprep[:, 0:wid], func=AF.Copy)
            st("scalar", p_wc, reads=["bufA", "hfm"], writes=[f"wp2_{c % 2}", "hfm"])
            def p_wo_(eng, name=name, c=c, wid=wid):
                eng.dma_start(out=scr[name][128 * c:128 * c + 128, :],
                              in_=wprep2[c % 2][:, 0:wid]).then_inc(sems["gpsimd"], 16)
            st("gpsimd", p_wo_, reads=[f"wp2_{c % 2}"], writes=[f"sc_{name}_{c}"], dma=1)

    for name, src_d, nch, wid in [("f1", dw["f1T"], NCH, FF), ("f2", dw["f2T"], NFH, DM)]:
        for c in range(nch):
            def p_wi(eng, src_d=src_d, c=c, wid=wid):
                eng.dma_start(out=wprep[:, 0:wid], in_=src_d[128 * c:128 * c + 128, :]).then_inc(sems["sync"], 16)
            st("sync", p_wi, writes=["bufA"], dma=1)
            def p_q(eng, c=c, wid=wid):
                qcg = xsq[:, :, :].rearrange("p c t -> p (c t)")
                eng.tensor_scalar(out=fprep[:, 0:wid], in0=wprep[:, 0:wid], scalar1=64.0,
                                  scalar2=MAGIC, op0=ALU.mult, op1=ALU.add)
                eng.tensor_copy(out=iprep[:, 0:wid], in_=fprep[:, 0:wid])
                eng.tensor_copy(out=wprep[:, 0:wid], in_=iprep[:, 0:wid])
                eng.tensor_tensor(out=qcg[:, 0:wid], in0=wprep[:, 0:wid],
                                  in1=fprep[:, 0:wid], op=ALU.is_gt)
                return eng.scalar_tensor_tensor(out=wprep2[c % 2][:, 0:wid], in0=wprep[:, 0:wid],
                                                scalar=-512.0, in1=qcg[:, 0:wid],
                                                op0=ALU.add, op1=ALU.subtract)
            st("vector", p_q, reads=["bufA", "bufB", "hfm", "xsq"],
               writes=[f"wp2_{c % 2}", "bufA", "bufB", "hfm", "xsq"])
            def p_wo_(eng, name=name, c=c, wid=wid):
                eng.dma_start(out=scr[name][128 * c:128 * c + 128, :],
                              in_=wprep2[c % 2][:, 0:wid]).then_inc(sems["gpsimd"], 16)
            st("gpsimd", p_wo_, reads=[f"wp2_{c % 2}"], writes=[f"sc_{name}_{c}"], dma=1)

    def p_zero(eng):
        eng.memset(ceps[:, :], EPS)
        eng.memset(ebuf[:, 512:544], 0.0)
        eng.memset(vst[:, :], 0.0)
        eng.memset(at2[:, :, :], 0.0)
        return eng.memset(avs[:, :], 0.0)
    st("vector", p_zero, writes=["vst", "at2p0", "at2p1", "avs", "kst", "qst",
                                 "y2s0", "y2s1", "xsq", "ceps", "ebuf"])

    scr_deps = {k: [f"sc_{k}_{c}" for c in range(nch)]
                for k, nch in [("wqkv", NCH), ("wo", NCH), ("w1", NCH),
                               ("w2", NFH), ("f1", NCH), ("f2", NFH)]}

    def w_view(slot, nch):
        return None

    def mm_phase(tag, scr_name, nqc, nch_c, n_ob, moving, psum_of, evac_fn, evac_eng,
                 extra_mm_reads, evac_reads_extra, evac_writes, group, post=None):
        ngrp = n_ob // group
        gw = group * 128
        for gi in range(ngrp):
            slot = gi % 2
            def p_dma(eng, gi=gi, slot=slot):
                eng.dma_start(
                    out=wqb[:, 4096 * slot: 4096 * slot + nch_c * gw].rearrange("p (c n) -> p c n", n=gw),
                    in_=scr[scr_name].rearrange("(c p) n -> p c n", p=128)[:, :, gw * gi: gw * gi + gw],
                ).then_inc(sems["gpsimd"], 16)
            rd = (scr_deps[scr_name] if gi < 2 else [])
            st("gpsimd", p_dma, reads=rd, writes=[f"wslot{slot}"], dma=1)
            for oj in range(group):
                ob = gi * group + oj
                psum, ptag = psum_of(ob)
                def p_mm(eng, gi=gi, slot=slot, oj=oj, psum=psum):
                    last = None
                    for c in range(nch_c):
                        last = eng.matmul(psum[:, :],
                                          wqb[:, 4096 * slot + c * gw + 128 * oj:
                                              4096 * slot + c * gw + 128 * oj + 128],
                                          moving[:, c, :], start=(c == 0), stop=(c == nch_c - 1))
                    return last
                st("tensor", p_mm, reads=[f"wslot{slot}"] + extra_mm_reads, writes=[ptag])
                def p_ev(eng, ob=ob, psum=psum):
                    return evac_fn(eng, ob, psum)
                st(evac_eng, p_ev, reads=[ptag] + evac_reads_extra, writes=evac_writes(ob))
                if post is not None:
                    post(ob)

    for it in range(NTILES):
        t0 = it * T

        def p_xin(eng, t0=t0):
            eng.dma_start(out=bufA[:, :, :],
                          in_=d_x.rearrange("(c p) t -> p c t", p=128)[:, :, t0:t0 + T]).then_inc(sems["sync"], 16)
        st("sync", p_xin, writes=["bufA"], dma=1)
        if it == 0: tap("x0", bufA[:, :, :].rearrange("p c t -> p (c t)"), ["bufA"], cols_f32=4096)

        def lnA_cast(eng):
            return eng.activation(out=xb[:, :, :], in_=bufA[:, :, :], func=AF.Copy)
        st("scalar", lnA_cast, reads=["bufA"], writes=["xb"])
        def lnA_sq(eng):
            return eng.tensor_mul(out=xsq[:, :, :], in0=xb[:, :, :], in1=xb[:, :, :])
        st("vector", lnA_sq, reads=["xb"], writes=["xsq"])
        def lnA_cs(eng):
            last = None
            for c in range(NCH):
                eng.matmul(pCS[0:1, :], onesc[:, :], xb[:, c, :], start=(c == 0), stop=(c == NCH - 1))
            for c in range(NCH):
                last = eng.matmul(pCS[32:33, :], onesc[:, :], xsq[:, c, :], start=(c == 0), stop=(c == NCH - 1))
            return last
        st("tensor", lnA_cs, reads=["xb", "xsq", "vecs"], writes=["pCS"])
        def lnA_rows(eng):
            eng.tensor_scalar_mul(out=rb(0), in0=pCS[0:1, :], scalar1=1.0 / DM)
            eng.tensor_scalar_mul(out=rb(1), in0=pCS[32:33, :], scalar1=1.0 / DM)
            eng.tensor_mul(out=rb(3), in0=rb(0), in1=rb(0))
            return eng.tensor_sub(out=rb(1), in0=rb(1), in1=rb(3))
        st("vector", lnA_rows, reads=["pCS"], writes=["rowbank"])
        def lnA_rsq(eng):
            eng.activation(out=rb(2), in_=rb(1), func=AF.Ln, bias=ceps[0:1, 0:1])
            eng.drain()
            return eng.activation(out=rb(2), in_=rb(2), func=AF.Exp, scale=-0.5)
        st("scalar", lnA_rsq, reads=["rowbank"], writes=["rowbank_r"])
        def lnA_rows2(eng):
            eng.tensor_mul(out=rb(3), in0=rb(0), in1=rb(2))
            eng.tensor_mul(out=rb(0), in0=rb(2), in1=rb(2))
            return eng.tensor_mul(out=rb(1), in0=rb(1), in1=rb(0))
        st("vector", lnA_rows2, reads=["rowbank_r"], writes=["rowbank_m"])
        def lnA_r2(eng):
            eng.activation(out=rb(1), in_=rb(1), func=AF.Ln, bias=ceps[0:1, 0:1])
            eng.drain()
            return eng.activation(out=rb(1), in_=rb(1), func=AF.Exp, scale=-0.5)
        st("scalar", lnA_r2, reads=["rowbank_m"], writes=["rowbank_r2"])
        def lnA_rows3(eng):
            eng.tensor_copy(out=rowsb[0:1, 0:T], in_=rb(2))
            return eng.tensor_copy(out=rowsb[0:1, T:2 * T], in_=rb(3))
        st("vector", lnA_rows3, reads=["rowbank_r2"], writes=["rowsb"])
        def lnA_bc1(eng):
            return eng.matmul(pBC0[:, :], onesr[0:1, :], rowsb[0:1, 0:T], start=True, stop=True)
        st("tensor", lnA_bc1, reads=["rowsb", "vecs"], writes=["pBC0"])
        def lnA_tt1(eng):
            last = None
            for c in range(NCH):
                last = eng.tensor_mul(out=bufB[:, c, :], in0=bufA[:, c, :], in1=pBC0[:, :])
            return last
        st("vector", lnA_tt1, reads=["pBC0", "bufA"], writes=["bufB"])
        def lnA_bc2(eng):
            return eng.matmul(pBC1[:, :], onesr[0:1, :], rowsb[0:1, T:2 * T], start=True, stop=True)
        st("tensor", lnA_bc2, reads=["rowsb", "vecs"], writes=["pBC1"])
        def lnA_tt2(eng):
            last = None
            for c in range(NCH):
                last = eng.tensor_sub(out=bufB[:, c, :], in0=bufB[:, c, :], in1=pBC1[:, :])
            return last
        st("vector", lnA_tt2, reads=["pBC1", "bufB"], writes=["bufB"])
        if it == 0: tap("u", bufB[:, :, :].rearrange("p c t -> p (c t)"), ["bufB"], cols_f32=4096)
        def lnA_rows4(eng):
            return eng.tensor_copy(out=rowsb[0:1, 0:T], in_=rb(1))
        st("vector", lnA_rows4, reads=["rowbank_r2", "pBC0"], writes=["rowsb"])
        def lnA_bc3(eng):
            return eng.matmul(pBC0[:, :], onesr[0:1, :], rowsb[0:1, 0:T], start=True, stop=True)
        st("tensor", lnA_bc3, reads=["rowsb", "vecs"], writes=["pBC0"])
        def lnA_tt3(eng):
            last = None
            for c in range(NCH):
                last = eng.tensor_mul(out=hbuf[:, c, :], in0=bufB[:, c, :], in1=pBC0[:, :])
            return last
        st("vector", lnA_tt3, reads=["pBC0", "bufB"], writes=["hbuf"])
        if it == 0: tap("h1", hbuf[:, :, :].rearrange("p c t -> p (c t)"), ["hbuf"], cols_bf=4096)

        def qkv_psum(ob):
            return (pA0, "pA0") if ob % 2 == 0 else (pA1, "pA1")
        def qkv_evac(eng, ob, psum):
            if ob < 8:
                dst, oc, sc = qfm, ob, SCALE_Q
            elif ob < 16:
                dst, oc, sc = kfm, ob - 8, 1.0
            else:
                dst, oc, sc = vfm, ob - 16, 1.0
            return eng.activation(out=dst[:, oc, :], in_=psum[:, :], func=AF.Identity,
                                  scale=sc, bias=biasq[:, ob:ob + 1])
        def qkv_ev_writes(ob):
            return ["qfm" if ob < 8 else ("kfm" if ob < 16 else "vfm")]
        mm_phase(f"qkv{it}", "wqkv", None, NCH, 32, hbuf, qkv_psum, qkv_evac, "scalar",
                 ["hbuf"], ["biasq"], qkv_ev_writes, group=4)
        if it == 0:
            tap("qfm", qfm[:, :, :].rearrange("p c t -> p (c t)"), ["qfm"], cols_bf=4096)
            tap("kfm", kfm[:, :, :].rearrange("p c t -> p (c t)"), ["kfm"], cols_bf=4096)
            tap("vfm", vfm[:, :, :].rearrange("p c t -> p (c t)"), ["vfm"], cols_bf=8192)

        def fold_qk(eng):
            last = None
            for (src, dst) in ((qfm, qst), (kfm, kst)):
                for tau in range(4):
                    for p in range(NCH):
                        for s in range(4):
                            i = 4 * p + s
                            last = eng.tensor_copy(
                                out=dst[32 * tau:32 * tau + 32, i:4096:32],
                                in_=src[32 * s:32 * s + 32, p, 128 * tau:128 * tau + 128])
            return last
        st("vector", fold_qk, reads=["qfm", "kfm"], writes=["qst", "kst"])
        if it == 0:
            tap("qst", qst[:, :], ["qst"], cols_bf=4096)
            tap("kst", kst[:, :], ["kst"], cols_bf=4096)
        def fold_v(eng):
            last = None
            for tau in range(4):
                for p in range(16):
                    for s in range(4):
                        dd = 4 * p + s
                        last = eng.tensor_copy(
                            out=vst[32 * tau:32 * tau + 32, dd:8192:64],
                            in_=vfm[32 * s:32 * s + 32, p, 128 * tau:128 * tau + 128])
            return last
        st("vector", fold_v, reads=["vfm", "vst"], writes=["vst"])
        if it == 0: tap("vst", vst[:, :], ["vst"], cols_bf=8192)

        for ab in range(8):
            par = ab % 2
            patt = pAT0 if par == 0 else pAT1
            ptag = "pAT0" if par == 0 else "pAT1"
            attag = "at2p0" if par == 0 else "at2p1"
            def a_mm(eng, ab=ab, patt=patt):
                last = None
                for gl in range(16):
                    g = 16 * ab + gl
                    for tau in range(4):
                        last = eng.matmul(
                            patt[32 * tau:32 * tau + 32, 32 * gl:32 * gl + 32],
                            qst[32 * tau:32 * tau + 32, 32 * g:32 * g + 32],
                            kst[32 * tau:32 * tau + 32, 32 * g:32 * g + 32],
                            start=True, stop=True, tile_position=(32 * tau, 32 * tau))
                return last
            st("tensor", a_mm, reads=["qst", "kst"], writes=[ptag])
            if it == 0 and ab == 0:
                tap("attp", pAT0[:, :], [ptag], cols_f32=512, psum=True)
            def a_sm1(eng, patt=patt):
                eng.reduce_max(out=statA[:, 0:16].unsqueeze(2),
                               in_=patt[:, :].rearrange("p (g j) -> p g j", j=32),
                               axis=AX.X, negate=True)
                eng.drain()
                return eng.tensor_tensor(
                    out=esub[:, :].rearrange("p (g j) -> p g j", j=32),
                    in0=patt[:, :].rearrange("p (g j) -> p g j", j=32),
                    in1=statA[:, 0:16].unsqueeze(2).broadcast_to([128, 16, 32]),
                    op=ALU.add)
            st("vector", a_sm1, reads=[ptag], writes=["y2s0", "statA"])
            def a_exp(eng):
                return eng.activation(out=ebuf[:, 0:512], in_=esub[:, :], func=AF.Exp)
            st("scalar", a_exp, reads=["y2s0"], writes=["ebuf"])
            if it == 0 and ab == 0:
                tap("ebuf", ebuf[:, 0:512], ["ebuf"], cols_bf=512)
            def a_sm2(eng):
                eng.reduce_sum(out=statA[:, 16:32].unsqueeze(2),
                               in_=ebuf[:, 0:512].rearrange("p (g j) -> p g j", j=32),
                               axis=AX.X)
                eng.drain()
                eng.reciprocal(out=statA[:, 16:32], in_=statA[:, 16:32])
                eng.drain()
                eng.tensor_tensor(out=ebuf[:, 0:512].rearrange("p (g j) -> p g j", j=32),
                                  in0=ebuf[:, 0:512].rearrange("p (g j) -> p g j", j=32),
                                  in1=statA[:, 16:32].unsqueeze(2).broadcast_to([128, 16, 32]),
                                  op=ALU.mult)
                eng.transpose(out=entb[:, :], in_=ebuf[:, 0:512])
                return eng.transpose(out=entb2[:, :], in_=ebuf[:, 16:528])
            st("vector", a_sm2, reads=["ebuf", "statA"], writes=["entb", "entb2", "ebuf", "statA"])
            if it == 0 and ab == 0:
                tap("entb", entb[:, :], ["entb"], cols_bf=512)
                tap("entb2", entb2[:, :], ["entb2"], cols_bf=512)
            def a_comb(eng, par=par):
                last = None
                for tau in range(4):
                    last = eng.scalar_tensor_tensor(
                        out=at2[32 * tau:32 * tau + 16, par, :].rearrange("p (g a) -> p g a", a=32)[:, :, 0:16],
                        in0=entb2[32 * tau:32 * tau + 16, :].rearrange("p (g i) -> p g i", i=32)[:, :, 16:32],
                        scalar=neglam[32 * tau:32 * tau + 16, :],
                        in1=entb[32 * tau:32 * tau + 16, :].rearrange("p (g i) -> p g i", i=32)[:, :, 0:16],
                        op0=ALU.mult, op1=ALU.add)
                return last
            st("vector", a_comb, reads=["entb", "entb2", "neglam"], writes=[attag])
            if it == 0 and ab == 0:
                tap("at2", at2[:, 0, :], [attag], cols_bf=512)

            for half in range(2):
                vb = 2 * ab + half
                pav = pA0 if vb % 2 == 0 else pA1
                pvtag = "pA0" if vb % 2 == 0 else "pA1"
                def v_mm(eng, ab=ab, vb=vb, pav=pav, par=par):
                    last = None
                    for gvl in range(8):
                        g = 8 * vb + gvl
                        gl = g - 16 * ab
                        for tau in range(4):
                            for e in range(2):
                                last = eng.matmul(
                                    pav[32 * tau:32 * tau + 32,
                                        64 * gvl + 32 * e:64 * gvl + 32 * e + 32],
                                    at2[32 * tau:32 * tau + 32, par, 32 * gl:32 * gl + 32],
                                    vst[32 * tau:32 * tau + 32,
                                        64 * g + 32 * e:64 * g + 32 * e + 32],
                                    start=True, stop=True, tile_position=(32 * tau, 32 * tau))
                    return last
                st("tensor", v_mm, reads=[attag, "vst"], writes=[pvtag])
                if it == 0 and ab == 0 and half == 0:
                    tap("avp", pA0[:, :], [pvtag], cols_f32=512, psum=True)
                def v_z2(eng, pav=pav):
                    return eng.activation(out=f32t[:, 0:512], in_=pav[:, :], func=AF.Square)
                st("scalar", v_z2, reads=[pvtag, "f32t"], writes=["f32t"])
                if it == 0 and ab == 0 and half == 0:
                    tap("z2t", f32t[:, 0:512], ["f32t"], cols_f32=512)
                def v_ss(eng):
                    c1 = (1.0 / 64.0) / (1.0 - LAMBDA_INIT) ** 2
                    c2 = EPS / (1.0 - LAMBDA_INIT) ** 2
                    eng.reduce_sum(out=statB[:, 0:8].unsqueeze(2),
                                   in_=f32t[:, 0:512].rearrange("p (g e) -> p g e", e=64),
                                   axis=AX.X)
                    eng.drain()
                    return eng.tensor_scalar(out=statB[:, 16:24], in0=statB[:, 0:8],
                                             scalar1=c1, scalar2=c2, op0=ALU.mult, op1=ALU.add)
                st("vector", v_ss, reads=["f32t"], writes=["statB"])
                def v_nrm(eng):
                    eng.activation(out=statB[:, 8:16], in_=statB[:, 16:24], func=AF.Ln)
                    eng.drain()
                    return eng.activation(out=statB[:, 8:16], in_=statB[:, 8:16],
                                          func=AF.Exp, scale=-0.5)
                st("scalar", v_nrm, reads=["statB"], writes=["statB_n"])
                def v_avs(eng, vb=vb, pav=pav):
                    return eng.tensor_tensor(
                        out=avs[:, 512 * vb:512 * vb + 512].rearrange("p (g e) -> p g e", e=64),
                        in0=pav[:, :].rearrange("p (g e) -> p g e", e=64),
                        in1=statB[:, 8:16].unsqueeze(2).broadcast_to([128, 8, 64]),
                        op=ALU.mult)
                st("vector", v_avs, reads=["statB_n", pvtag], writes=["avs"])
                if it == 0 and ab == 0 and half == 0:
                    tap("nrm", statB[:, 0:24], ["statB_n"], cols_f32=24)

        def refold(eng):
            last = None
            for dd in range(64):
                p, s = divmod(dd, 4)
                for tau in range(4):
                    last = eng.tensor_copy(
                        out=omfm[32 * s:32 * s + 32, p, 128 * tau:128 * tau + 128],
                        in_=avs[32 * tau:32 * tau + 32, dd:8192:64])
            return last
        if it == 0: tap("avs", avs[:, :], ["avs"], cols_bf=8192)
        st("vector", refold, reads=["avs", "vfm"], writes=["vfm"])
        if it == 0: tap("om", vfm[:, :, :].rearrange("p c t -> p (c t)"), ["vfm"], cols_bf=8192)

        def wo_psum(ob):
            return (pA0, "pA0") if ob % 2 == 0 else (pA1, "pA1")
        def wo_evac(eng, ob, psum):
            return eng.scalar_tensor_tensor(out=bufB[:, ob, :], in0=psum[:, :],
                                            scalar=vcol["bo"][:, ob:ob + 1],
                                            in1=bufB[:, ob, :], op0=ALU.add, op1=ALU.add)
        mm_phase(f"wo{it}", "wo", None, 16, NCH, omfm, wo_psum, wo_evac, "vector",
                 ["vfm"], ["vecs", "bufB"], lambda ob: ["bufB"], group=2)

        if it == 0: tap("c", bufB[:, :, :].rearrange("p c t -> p (c t)"), ["bufB"], cols_f32=4096)
        def mk_p(eng):
            last = None
            for c in range(NCH):
                last = eng.tensor_add(out=bufA[:, c, :], in0=bufA[:, c, :], in1=bufB[:, c, :])
            return last
        st("vector", mk_p, reads=["bufA", "bufB"], writes=["bufA"])
        if it == 0: tap("p", bufA[:, :, :].rearrange("p c t -> p (c t)"), ["bufA"], cols_f32=4096)

        def lnC_cast(eng):
            return eng.activation(out=xb[:, :, :], in_=bufB[:, :, :], func=AF.Copy)
        st("scalar", lnC_cast, reads=["bufB", "xb"], writes=["xb"])
        def lnC_sq(eng):
            return eng.tensor_mul(out=xsq[:, :, :], in0=xb[:, :, :], in1=xb[:, :, :])
        st("vector", lnC_sq, reads=["xb"], writes=["xsq"])
        def lnC_cs(eng):
            last = None
            for c in range(NCH):
                eng.matmul(pCS[0:1, :], onesc[:, :], xb[:, c, :], start=(c == 0), stop=(c == NCH - 1))
            for c in range(NCH):
                last = eng.matmul(pCS[32:33, :], onesc[:, :], xsq[:, c, :], start=(c == 0), stop=(c == NCH - 1))
            return last
        st("tensor", lnC_cs, reads=["xb", "xsq"], writes=["pCS"])
        def lnC_rows(eng):
            eng.tensor_scalar_mul(out=rb(0), in0=pCS[0:1, :], scalar1=1.0 / DM)
            eng.tensor_scalar_mul(out=rb(1), in0=pCS[32:33, :], scalar1=1.0 / DM)
            eng.tensor_mul(out=rb(3), in0=rb(0), in1=rb(0))
            return eng.tensor_sub(out=rb(1), in0=rb(1), in1=rb(3))
        st("vector", lnC_rows, reads=["pCS"], writes=["rowbank"])
        def lnC_rsq(eng):
            eng.activation(out=rb(2), in_=rb(1), func=AF.Ln, bias=ceps[0:1, 0:1])
            eng.drain()
            return eng.activation(out=rb(2), in_=rb(2), func=AF.Exp, scale=-0.5)
        st("scalar", lnC_rsq, reads=["rowbank"], writes=["rowbank_r"])
        def lnC_rows2(eng):
            eng.tensor_mul(out=rb(3), in0=rb(0), in1=rb(2))
            eng.tensor_copy(out=rowsb[0:1, 0:T], in_=rb(2))
            return eng.tensor_copy(out=rowsb[0:1, T:2 * T], in_=rb(3))
        st("vector", lnC_rows2, reads=["rowbank_r"], writes=["rowsb"])
        def lnC_bc1(eng):
            return eng.matmul(pBC0[:, :], onesr[0:1, :], rowsb[0:1, 0:T], start=True, stop=True)
        st("tensor", lnC_bc1, reads=["rowsb", "vecs"], writes=["pBC0"])
        def lnC_bc2(eng):
            return eng.matmul(pBC1[:, :], onesr[0:1, :], rowsb[0:1, T:2 * T], start=True, stop=True)
        st("tensor", lnC_bc2, reads=["rowsb", "vecs"], writes=["pBC1"])
        def lnC_tt(eng):
            last = None
            for c in range(NCH):
                eng.tensor_mul(out=esub[:, :], in0=bufB[:, c, :], in1=pBC0[:, :])
                last = eng.tensor_sub(out=hbuf[:, c, :], in0=esub[:, :], in1=pBC1[:, :])
            return last
        st("vector", lnC_tt, reads=["pBC0", "pBC1", "bufB", "hbuf", "y2s0"],
           writes=["hbuf", "y2s0"])
        if it == 0: tap("h2", hbuf[:, :, :].rearrange("p c t -> p (c t)"), ["hbuf"], cols_bf=4096)

        def w1_psum(ob):
            return (pA0, "pA0") if ob % 2 == 0 else (pA1, "pA1")
        def w1_evac(eng, ob, psum):
            return eng.activation(out=hfm[:, ob, :], in_=psum[:, :], func=AF.Relu,
                                  bias=bias1[:, ob:ob + 1])
        mm_phase(f"w1{it}", "w1", None, NCH, NFH, hbuf, w1_psum, w1_evac, "scalar",
                 ["hbuf"], ["bias1"], lambda ob: ["hfm"], group=4)
        if it == 0: tap("hfm", hfm[:, :, :].rearrange("p c t -> p (c t)"), ["hfm"], cols_bf=16384)
        def w2_psum(ob):
            return (pA0, "pA0") if ob % 2 == 0 else (pA1, "pA1")
        def w2_evac(eng, ob, psum):
            return eng.scalar_tensor_tensor(out=bufB[:, ob, :], in0=psum[:, :],
                                            scalar=vcol["b2a"][:, ob:ob + 1],
                                            in1=bufA[:, ob, :], op0=ALU.add, op1=ALU.add)
        mm_phase(f"w2{it}", "w2", None, NFH, NCH, hfm, w2_psum, w2_evac, "vector",
                 ["hfm"], ["vecs", "bufA", "bufB"], lambda ob: ["bufB"], group=1)

        if it == 0: tap("x1", bufB[:, :, :].rearrange("p c t -> p (c t)"), ["bufB"], cols_f32=4096)
        def lnD_cast(eng):
            return eng.activation(out=xb[:, :, :], in_=bufB[:, :, :], func=AF.Copy)
        st("scalar", lnD_cast, reads=["bufB", "xb"], writes=["xb"])
        def lnD_sq(eng):
            return eng.tensor_mul(out=xsq[:, :, :], in0=xb[:, :, :], in1=xb[:, :, :])
        st("vector", lnD_sq, reads=["xb"], writes=["xsq"])
        def lnD_cs(eng):
            last = None
            for c in range(NCH):
                eng.matmul(pCS[0:1, :], onesc[:, :], xb[:, c, :], start=(c == 0), stop=(c == NCH - 1))
            for c in range(NCH):
                last = eng.matmul(pCS[32:33, :], onesc[:, :], xsq[:, c, :], start=(c == 0), stop=(c == NCH - 1))
            return last
        st("tensor", lnD_cs, reads=["xb", "xsq"], writes=["pCS"])
        def lnD_rows(eng):
            eng.tensor_scalar_mul(out=rb(0), in0=pCS[0:1, :], scalar1=1.0 / DM)
            eng.tensor_scalar_mul(out=rb(1), in0=pCS[32:33, :], scalar1=1.0 / DM)
            eng.tensor_mul(out=rb(3), in0=rb(0), in1=rb(0))
            return eng.tensor_sub(out=rb(1), in0=rb(1), in1=rb(3))
        st("vector", lnD_rows, reads=["pCS"], writes=["rowbank"])
        def lnD_rsq(eng):
            eng.activation(out=rb(2), in_=rb(1), func=AF.Ln, bias=ceps[0:1, 0:1])
            eng.drain()
            return eng.activation(out=rb(2), in_=rb(2), func=AF.Exp, scale=-0.5)
        st("scalar", lnD_rsq, reads=["rowbank"], writes=["rowbank_r"])
        def lnD_rows2(eng):
            eng.tensor_mul(out=rb(3), in0=rb(0), in1=rb(2))
            eng.tensor_scalar(out=rb(3), in0=rb(3), scalar1=16.0,
                              scalar2=MAGIC, op0=ALU.mult, op1=ALU.subtract)
            eng.tensor_copy(out=rowsb[0:1, 0:T], in_=rb(2))
            return eng.tensor_copy(out=rowsb[0:1, T:2 * T], in_=rb(3))
        st("vector", lnD_rows2, reads=["rowbank_r"], writes=["rowsb"])
        def lnD_bc1(eng):
            return eng.matmul(pBC0[:, :], onesr[0:1, :], rowsb[0:1, 0:T], start=True, stop=True)
        st("tensor", lnD_bc1, reads=["rowsb", "vecs"], writes=["pBC0"])
        def lnD_bc2(eng):
            return eng.matmul(pBC1[:, :], onesr[0:1, :], rowsb[0:1, T:2 * T], start=True, stop=True)
        st("tensor", lnD_bc2, reads=["rowsb", "vecs"], writes=["pBC1"])
        def mk_xq(eng):
            last = None
            for c in range(NCH):
                eng.tensor_mul(out=y2s[0][:, :], in0=bufB[:, c, :], in1=pBC0[:, :])
                eng.scalar_tensor_tensor(out=y2s[0][:, :], in0=y2s[0][:, :], scalar=16.0,
                                         in1=pBC1[:, :], op0=ALU.mult, op1=ALU.subtract)
                eng.tensor_copy(out=i32a[:, :], in_=y2s[0][:, :])
                eng.tensor_copy(out=f32a[:, :], in_=i32a[:, :])
                eng.tensor_tensor(out=cgta[:, :], in0=f32a[:, :], in1=y2s[0][:, :], op=ALU.is_gt)
                eng.scalar_tensor_tensor(out=f32a[:, :], in0=f32a[:, :], scalar=-512.0,
                                         in1=cgta[:, :], op0=ALU.add, op1=ALU.subtract)
                last = eng.tensor_scalar(out=hbuf[:, c, :], in0=f32a[:, :],
                                         scalar1=127.0, scalar2=-128.0, op0=ALU.min, op1=ALU.max)
            return last
        st("vector", mk_xq, reads=["pBC0", "pBC1", "bufB", "hbuf", "y2s0", "y2s1", "i32t", "f32t"],
           writes=["hbuf", "y2s0", "i32t", "y2s1", "f32t"])

        if it == 0: tap("xq", hbuf[:, :, :].rearrange("p c t -> p (c t)"), ["hbuf"], cols_bf=4096)
        def f1_psum(ob):
            return (pA0, "pA0") if ob % 2 == 0 else (pA1, "pA1")
        def f1_evac(eng, ob, psum):
            return eng.activation(out=y2s[ob % 2][:, :], in_=psum[:, :], func=AF.Identity,
                                  bias=f1bias[:, ob:ob + 1])
        def post_hq(ob):
            def mk_hq(eng, ob=ob):
                yb = y2s[ob % 2]
                eng.tensor_copy(out=i32a[:, :], in_=yb[:, :])
                eng.tensor_single_scalar(out=i32b[:, :], in_=i32a[:, :], scalar=6,
                                         op=ALU.arith_shift_right)
                eng.tensor_single_scalar(out=i32b[:, :], in_=i32b[:, :], scalar=1,
                                         op=ALU.bitwise_and)
                eng.scalar_tensor_tensor(out=i32b[:, :], in0=i32a[:, :], scalar=31,
                                         in1=i32b[:, :], op0=ALU.add, op1=ALU.add)
                eng.tensor_single_scalar(out=i32b[:, :], in_=i32b[:, :], scalar=6,
                                         op=ALU.arith_shift_right)
                return eng.tensor_scalar(out=hfm[:, ob, :], in0=i32b[:, :],
                                         scalar1=127, scalar2=0, op0=ALU.min, op1=ALU.max)
            st("vector", mk_hq, reads=[f"y2s{ob % 2}", "hfm", "i32t"], writes=["hfm", "i32t"])
        mm_phase(f"f1{it}", "f1", None, NCH, NFH, hbuf, f1_psum, f1_evac, "scalar",
                 ["hbuf"], ["fbias"], lambda ob: [f"y2s{ob % 2}"], group=4, post=post_hq)

        if it == 0: tap("hq", hfm[:, :, :].rearrange("p c t -> p (c t)"), ["hfm"], cols_bf=16384)
        def f2_psum(ob):
            return (pA0, "pA0") if ob % 2 == 0 else (pA1, "pA1")
        def f2_evac(eng, ob, psum):
            return eng.activation(out=y2s[ob % 2][:, :], in_=psum[:, :], func=AF.Identity,
                                  scale=2.0 ** -10, bias=f2bias[:, ob:ob + 1])
        def post_x2(ob):
            def mk_x2(eng, ob=ob):
                return eng.tensor_add(out=bufA[:, ob, :], in0=y2s[ob % 2][:, :], in1=bufB[:, ob, :])
            st("vector", mk_x2, reads=[f"y2s{ob % 2}", "bufB", "bufA"], writes=["bufA"])
        mm_phase(f"f2{it}", "f2", None, NFH, NCH, hfm, f2_psum, f2_evac, "scalar",
                 ["hfm"], ["fbias"], lambda ob: [f"y2s{ob % 2}"], group=1, post=post_x2)

        if it == 0: tap("x2", bufA[:, :, :].rearrange("p c t -> p (c t)"), ["bufA"], cols_f32=4096)
        def p_xout(eng, t0=t0):
            eng.dma_start(out=d_out.rearrange("(c p) t -> p c t", p=128)[:, :, t0:t0 + T],
                          in_=bufA[:, :, :]).then_inc(sems["sync"], 16)
        st("sync", p_xout, reads=["bufA"], writes=["xout"], dma=1)

    final_sync_count = S.count["sync"]
    def p_final(eng):
        eng.wait_ge(sems["sync"], final_sync_count)
        return None
    st("gpsimd", p_final, reads=["xout"])

    with es:
        for e in Sched.ENGS:
            sems[e] = es.enter_context(nc.semaphore("sem_" + e))
        blk = es.enter_context(nc.Block())
        run = S.make_runner(sems)
        blk.tensor(lambda eng: run("tensor", eng))
        blk.vector(lambda eng: run("vector", eng))
        blk.scalar(lambda eng: run("scalar", eng))
        blk.sync(lambda eng: run("sync", eng))
        blk.gpsimd(lambda eng: run("gpsimd", eng))
    return nc


_NC_CACHE = None

def kernel(**inputs):
    global _NC_CACHE
    shared, per_core = _stage(inputs)
    if _NC_CACHE is None:
        _NC_CACHE = build_nc()
    nc = _NC_CACHE
    in_maps = []
    for c in range(NCORES):
        m = dict(shared)
        m["xfm"] = per_core[c]["xfm"]
        in_maps.append(m)
    res = run_bass_kernel_spmd(nc, in_maps, core_ids=list(range(NCORES)))
    out = np.empty((NTOK, DM), dtype=np.float32)
    for c in range(NCORES):
        out[c * TPC:(c + 1) * TPC, :] = np.asarray(res.results[c]["outfm"]).T
    return out.reshape(4, 8192, DM)
